# version 09
# baseline (speedup 1.0000x reference)
import math
import os
from contextlib import ExitStack

import numpy as np
import ml_dtypes

import concourse.bass as bass
import concourse.mybir as mybir
from concourse.bacc import Bacc
from concourse.bass_utils import run_bass_kernel_spmd

F32 = mybir.dt.float32
BF16 = mybir.dt.bfloat16
I16 = mybir.dt.int16
I32 = mybir.dt.int32
U32 = mybir.dt.uint32
AF = mybir.ActivationFunctionType
ALU = mybir.AluOpType
AX = mybir.AxisListType


B, S, D = 16, 2048, 1024
NCORES = 8
BPC = B // NCORES
TPC = BPC * S
NTOK = B * S
H, DH, DHR = 16, 64, 32
DLAT = 512
DQK = DH + DHR
E, TOPK = 32, 4
EPC = E // NCORES
CAP = int(1.25 * NTOK * TOPK / E)
CAPJ = CAP // 128
INTER = 1024
EPS = 1e-5
ALPHA = 1.0 / math.sqrt(float(DQK))
CLO = math.exp(-80.0 * ALPHA)
CHI = math.exp(80.0 * ALPHA)
NCH = TPC // 128
SCH = S // 128
WW = E
ROWW = D + WW
RW2 = ROWW // 2
TRASH = NTOK


class SemT:
    def __init__(self, ctx, nc, name):
        self.h = ctx.enter_context(nc.semaphore(name))
        self.n = 0

    def inc(self, instr, k=1):
        instr.then_inc(self.h, k)
        self.n += k
        return instr


def build_program(dbg=(), stages=9):
    nc = Bacc("TRN2", target_bir_lowering=False, detect_race_conditions=False)
    root = ExitStack()

    def dram(name, shape, dtype, kind="Internal"):
        if name in dbg:
            kind = "ExternalOutput"
        return nc.dram_tensor(name, shape, dtype, kind=kind)

    def ein(name, shape, dtype=F32):
        return nc.dram_tensor(name, shape, dtype, kind="ExternalInput")

    x_in = ein("x", [TPC, D])
    wdq_t = ein("wdq_t", [D, DLAT])
    wdkv_t = ein("wdkv_t", [D, DLAT])
    wkr_t = ein("wkr_t", [D, DHR])
    wuq_t = ein("wuq_t", [DLAT, H * DH])
    wqr_t = ein("wqr_t", [DLAT, H * DHR])
    wuk_t = ein("wuk_t", [DLAT, H * DH])
    wuv_t = ein("wuv_t", [DLAT, H * DH])
    wo_t = ein("wo_t", [H * DH, D])
    wg_t = ein("wg_t", [D, E])
    sw1_t = ein("sw1", [D, INTER])
    sw3_t = ein("sw3", [D, INTER])
    sw2_t = ein("sw2", [INTER, D])
    w1e = ein("w1e", [EPC, D, INTER])
    w3e = ein("w3e", [EPC, D, INTER])
    w2e = ein("w2e", [EPC, INTER, D])
    cosq_t = ein("cosq", [128, S], BF16)
    sinq_t = ein("sinq", [128, S], BF16)
    trimask = ein("trimask", [128, 128], BF16)
    onesf = ein("onesf", [1, 128])
    onescol_t = ein("onescol", [128, 1])
    slotiota = ein("slotiota", [128, CAPJ])
    esel_t = ein("esel", [128, EPC * E])
    identf_t = ein("identf", [128, 128])
    irow_t = ein("irow", [128, 256])
    brow128_t = ein("brow128", [1, 128])

    out_t = nc.dram_tensor("out", [TPC, D], F32, kind="ExternalOutput")

    xa_dram = dram("xa_dram", [S, D], BF16)
    x2_dram = dram("x2_dram", [TPC, D], F32)
    xnw_own = dram("xnw_own", [TPC, ROWW], BF16)
    xnw_all = dram("xnw_all", [NTOK + 1, ROWW], BF16)
    wsel_own = dram("wsel_own", [TPC, E], F32)
    wsel_all = dram("wsel_all", [NTOK, E], F32)
    z_dram = dram("z_dram", [TPC, D], BF16)
    xe_dram = dram("xe_dram", [EPC, CAP, D], BF16)
    y_acc = dram("y_acc", [NTOK + 1, D], BF16)
    y_own = dram("y_own", [TPC, D], BF16)
    dbg_idx = dram("dbg_idx", [128, EPC * CAPJ], I32) if "dbg_idx" in dbg else None

    rg = [list(range(NCORES))]

    sbr = lambda n_, s_, d_: root.enter_context(nc.sbuf_tensor(n_, s_, d_))
    semr = lambda n_: SemT(root, nc, n_)

    s_tri = sbr("s_tri", [128, 128], BF16)
    s_ones = sbr("s_ones", [1, 128], F32)
    s_stat = sbr("s_stat", [128, 12], F32)
    s_dump = sbr("s_dump", [128, 1024], F32)
    s_zero = sbr("s_zero", [128, 1024], BF16)
    s_rrow = sbr("s_rrow", [1, 512], F32)
    s_onescol = sbr("s_onescol", [128, 1], F32)
    s_rb = [sbr("s_rb0", [64, 512], BF16), sbr("s_rb1", [64, 512], BF16)]

    p_a = root.enter_context(nc.psum_tensor("p_a", [128, 512], F32))
    p_b = root.enter_context(nc.psum_tensor("p_b", [128, 512], F32))
    p_s0 = root.enter_context(nc.psum_tensor("p_s0", [128, 512], F32))
    p_s1 = root.enter_context(nc.psum_tensor("p_s1", [128, 512], F32))
    p_at0 = root.enter_context(nc.psum_tensor("p_at0", [65, 512], F32))
    p_at1 = root.enter_context(nc.psum_tensor("p_at1", [65, 512], F32))
    p_bc = root.enter_context(nc.psum_tensor("p_bc", [128, 512], F32))

    block_cm = nc.Block()
    block = block_cm.__enter__()

    s_guard = sbr("s_guard", [128, 32], F32)
    _hz_last = {"v": [frozenset(), frozenset()], "s": [frozenset(), frozenset()]}

    class _HzEng:
        def __init__(self, eng, key):
            self._e = eng
            self._k = key

        def wait_ge(self, *a, **kw):
            h = _hz_last[self._k]
            _hz_last[self._k] = [frozenset(), h[0]]
            return self._e.wait_ge(*a, **kw)

        def _spacer(self):
            if self._k == "v":
                self._e.memset(s_guard[:, 0:16], 0.0)
            else:
                self._e.activation(s_guard[:, 16:32], s_guard[:, 16:32], AF.Copy)
            h = _hz_last[self._k]
            _hz_last[self._k] = [frozenset(), h[0]]

        def __getattr__(self, nm):
            fn = getattr(self._e, nm)
            if not callable(fn) or nm.startswith("_") or nm in (
                    "add_instruction", "alloc_register", "value_load"):
                return fn

            def call(*a, **kw):
                wr = set()
                rd = set()
                for i, v in enumerate(a):
                    if isinstance(v, bass.AP):
                        (wr if i == 0 else rd).add(v.name)
                for kk, v in kw.items():
                    if isinstance(v, bass.AP):
                        (wr if kk in ("out", "accum_out", "num_found") else rd).add(v.name)
                h = _hz_last[self._k]
                while rd & (h[0] | h[1]):
                    self._spacer()
                    h = _hz_last[self._k]
                r = fn(*a, **kw)
                _hz_last[self._k] = [frozenset(wr), h[0]]
                return r
            return call

    class _HzBlock:
        def __init__(self, b):
            self._b = b

        def vector(self, fn):
            return self._b.vector(lambda e: fn(_HzEng(e, "v")))

        def scalar(self, fn):
            return self._b.scalar(lambda e: fn(_HzEng(e, "s")))

        def __getattr__(self, nm):
            return getattr(self._b, nm)

    block = _HzBlock(block)

    BAR = semr("bar")
    ld0 = semr("ld0")
    zro = semr("zro")
    ldx = semr("ldx")
    gld = semr("gld")
    vst = semr("vst")
    sst = semr("sst")
    tmm = semr("tmm")
    dst = semr("dst")
    trx = semr("trx")
    smm = semr("smm")
    sexp = semr("sexp")
    sclp = semr("sclp")
    spv = semr("spv")
    sden = semr("sden")
    sbouter = semr("sbouter")
    sbc = semr("sbc")
    satv = semr("satv")
    ccs = semr("ccs")
    gth = semr("gth")
    sca = semr("sca")
    sout = semr("sout")
    qcp = semr("qcp")
    wsb = semr("wsb")
    scg = semr("scg")
    lnv = semr("lnv")
    lns = semr("lns")

    def barrier():
        BAR.n += 5
        tgt = BAR.n
        for name in ("tensor", "vector", "scalar", "gpsimd", "sync"):
            def _f(eng, t=tgt):
                eng.sem_inc(BAR.h, 1)
                eng.wait_ge(BAR.h, t)
            getattr(block, name)(_f)

    def _quiesce():
        tgt = [(s.h, s.n) for s in (dst, trx, ldx, gld, zro, gth, sca, qcp, wsb, ccs, sout)]
        def _f(eng, _n):
            for h, v in tgt:
                if v:
                    eng.wait_ge(h, v)
        for name in ("tensor", "vector", "scalar", "gpsimd", "sync"):
            getattr(block, name)(lambda eng, _n=name: _f(eng, _n))
        block_cm.__exit__(None, None, None)
        nc._keepalive = root
        return nc

    LNK = [0]

    def emit_ln(parts, pre_wait_v=None, pre_wait_s=None):
        np_ = len(parts)
        assert np_ <= 3
        lns_base = lns.n
        lnv_base = lnv.n
        totw = sum(p.shape[-1] for p in parts)

        def _s1(sc):
            if pre_wait_s is not None:
                pre_wait_s(sc)
            sc.wait_ge(lnv.h, lnv_base)
            nn_ = 0
            for i, p in enumerate(parts):
                w = p.shape[-1]
                nsp = (w + 511) // 512
                for ss in range(nsp):
                    w0 = ss * 512
                    w1 = min(w, w0 + 512)
                    do = (nn_ % 2) * 512
                    lns.inc(sc.activation(s_dump[:, do:do + (w1 - w0)], p[:, w0:w1], AF.Square,
                                          accum_out=s_stat[:, 4 + nn_:5 + nn_]), 1)
                    nn_ += 1
        block.scalar(_s1)
        n_sq = lns.n - lns_base

        def _v1(ve):
            if pre_wait_v is not None:
                pre_wait_v(ve)
            ve.wait_ge(lns.h, lns_base + n_sq)
            nred = 0
            for i, p in enumerate(parts):
                w = p.shape[-1]
                nsp = (w + 511) // 512
                for ss in range(nsp):
                    w0 = ss * 512
                    w1 = min(w, w0 + 512)
                    ve.tensor_reduce(s_stat[:, 8 + nred:9 + nred] if nred < 4 else s_stat[:, 8:9],
                                     p[:, w0:w1], axis=AX.X, op=ALU.add)
                    nred += 1
            for j in range(1, n_sq):
                ve.tensor_add(s_stat[:, 8:9], s_stat[:, 8:9], s_stat[:, 8 + j:9 + j])
                ve.tensor_add(s_stat[:, 4:5], s_stat[:, 4:5], s_stat[:, 4 + j:5 + j])
            ve.tensor_scalar(s_stat[:, 0:1], s_stat[:, 8:9], -1.0 / totw, None, ALU.mult)
            ve.tensor_scalar(s_stat[:, 4:5], s_stat[:, 4:5], 1.0 / totw, None, ALU.mult)
            ve.tensor_tensor(s_stat[:, 9:10], s_stat[:, 0:1], s_stat[:, 0:1], ALU.mult)
            ve.tensor_sub(s_stat[:, 4:5], s_stat[:, 4:5], s_stat[:, 9:10])
            lnv.inc(ve.tensor_scalar(s_stat[:, 4:5], s_stat[:, 4:5], EPS, None, ALU.add), 1)
        block.vector(_v1)
        assert n_sq <= 4

        def _s2(sc):
            sc.wait_ge(lnv.h, lnv_base + 1)
            lns.inc(sc.activation(s_stat[:, 9:10], s_stat[:, 4:5], AF.Sqrt), 1)
        block.scalar(_s2)

        def _v2(ve):
            ve.wait_ge(lns.h, lns_base + n_sq + 1)
            lnv.inc(ve.reciprocal(s_stat[:, 1:2], s_stat[:, 9:10]), 1)
        block.vector(_v2)
        LNK[0] += 1

    def p0_sync(sy):
        ld0.inc(sy.dma_start(out=s_tri[:, :], in_=trimask[:, :]), 16)
        ld0.inc(sy.dma_start(out=s_ones[:, :], in_=onesf[:, :]), 16)
        ld0.inc(sy.dma_start(out=s_onescol[:, :], in_=onescol_t[:, :]), 16)
    block.sync(p0_sync)

    def p0_v(ve):
        vst.inc(ve.memset(s_zero[:, :], 0.0), 1)
    block.vector(p0_v)
    VST_ZMEM = vst.n

    attn_ctx = ExitStack()
    sba = lambda n_, s_, d_: attn_ctx.enter_context(nc.sbuf_tensor(n_, s_, d_))

    s_cosq = sba("s_cosq", [128, S], BF16)
    s_sinq = sba("s_sinq", [128, S], BF16)

    def p0_g(gp):
        gld.inc(gp.dma_start(out=s_cosq[:, :], in_=cosq_t[:, :]), 16)
        gld.inc(gp.dma_start(out=s_sinq[:, :], in_=sinq_t[:, :]), 16)
    block.gpsimd(p0_g)

    s_cq = sba("s_cq", [128, 4 * S], BF16)
    s_ckv = sba("s_ckv", [128, 4 * S], BF16)
    s_vtp = [sba("s_vtp0", [128, SCH * 130], BF16), sba("s_vtp1", [128, SCH * 130], BF16)]
    s_kr1 = sba("s_kr1", [16, S], BF16)
    s_kr2 = sba("s_kr2", [16, S], BF16)
    s_qr4 = sba("s_qr4", [128, S], BF16)
    s_krr1 = s_qr4[0:16, :]

    s_qr4r = [sba("s_qr4r0", [128, S], BF16)] * 2
    s_krr2 = s_qr4r[0][0:16, :]
    s_qh = [sba(f"s_qh{i}", [96, S], BF16) for i in range(4)]
    s_kh = [sba(f"s_kh{i}", [96, S], BF16) for i in range(4)]
    s_praw = [sba(f"s_praw{i}", [128, 512], BF16) for i in range(3)]
    s_p = [sba(f"s_p{i}", [128, 512], BF16) for i in range(3)]
    s_afm = sba("s_afm", [128, 8 * S], BF16)
    s_upq = [sba(f"s_upq{i}", [128, 4 * 128], BF16) for i in range(2)]
    s_upk = [sba(f"s_upk{i}", [128, 4 * 128], BF16) for i in range(2)]
    s_upv = [sba(f"s_upv{i}", [128, 4 * 128], BF16) for i in range(2)]
    s_uqr = [sba(f"s_uqr{i}", [128, 4 * 128], BF16) for i in range(2)]
    s_x = [sba("s_x0", [128, D], F32), sba("s_x1", [128, D], F32)]
    s_xa = sba("s_xa", [128, D], BF16)
    s_ao = sba("s_ao", [128, D], F32)
    s_x2c = sba("s_x2c", [128, D], F32)
    s_xnc = sba("s_xnc", [128, D], BF16)
    s_xafm = sba("s_xafm", [128, 8 * S], BF16)

    CQ = s_cq[:, :].rearrange("p (c s) -> p c s", s=S)
    CKV = s_ckv[:, :].rearrange("p (c s) -> p c s", s=S)
    VTP = [t[:, :].rearrange("p (c h) -> p c h", h=130) for t in s_vtp]
    AFm = s_afm[:, :].rearrange("p (c s) -> p c s", s=S)
    XF = s_xafm[:, :].rearrange("p (c s) -> p c s", s=S)
    UPQ = [t[:, :].rearrange("p (c l) -> p c l", l=128) for t in s_upq]
    UPK = [t[:, :].rearrange("p (c l) -> p c l", l=128) for t in s_upk]
    UPV = [t[:, :].rearrange("p (c l) -> p c l", l=128) for t in s_upv]
    UQR = [t[:, :].rearrange("p (c l) -> p c l", l=128) for t in s_uqr]
    WDQ = s_afm[:, 0:8 * DLAT].rearrange("p (c l) -> p c l", l=DLAT)
    WDKV = s_afm[:, 8 * DLAT:16 * DLAT].rearrange("p (c l) -> p c l", l=DLAT)
    WKR = s_afm[:, 16 * DLAT:16 * DLAT + 8 * DHR].rearrange("p (c l) -> p c l", l=DHR)
    WO = s_xafm[:, 8 * S // 2:].rearrange("p (c l) -> p c l", l=D)

    TILES = []
    AJL = []
    for b in range(BPC):
        for h in range(H):
            for j in range(4):
                AJL.append((b, h, j))
                for kt in range(4 * j + 4):
                    TILES.append((b, h, j, kt))
    TIDX = {t: i for i, t in enumerate(TILES)}
    AIDX = {a: i for i, a in enumerate(AJL)}
    aj_last_tile = {}
    for (b, h, j, kt) in TILES:
        aj_last_tile[(b, h, j)] = TIDX[(b, h, j, kt)]
    pair_last_tile = {}
    for (b, h, j, kt) in TILES:
        pair_last_tile[(b, h // 2)] = TIDX[(b, h, j, kt)]

    NAJ = len(AJL)


    for b in range(BPC):
        TMM_PRE_B = tmm.n

        def aw_ld(gp, b=b, TMM_PRE_B=TMM_PRE_B):
            if b == 1:
                gp.wait_ge(tmm.h, TMM_PRE_B)
            for dstv, srct in ((WDQ, wdq_t), (WDKV, wdkv_t), (WKR, wkr_t)):
                gld.inc(gp.dma_start(
                    out=dstv, in_=srct.ap().rearrange("(c p) l -> p c l", p=128)), 16)
        block.gpsimd(aw_ld)
        GLD_W = gld.n

        lna_done = []
        sta_done = []
        for ch in range(SCH):
            def a1_ld(sy, ch=ch, b=b):
                if ch >= 2:
                    sy.wait_ge(lnv.h, lna_done[ch - 2])
                ldx.inc(sy.dma_start(out=s_x[ch % 2][:, :],
                                     in_=x_in[b * S + ch * 128:b * S + (ch + 1) * 128, :]), 16)
            block.sync(a1_ld)
            LDX_T = ldx.n

            emit_ln([s_x[ch % 2][:, :]],
                    pre_wait_v=lambda ve, t=LDX_T: ve.wait_ge(ldx.h, t),
                    pre_wait_s=lambda sc, t=LDX_T: sc.wait_ge(ldx.h, t))

            def a1_ap(ve, ch=ch):
                if ch >= 1:
                    ve.wait_ge(dst.h, sta_done[ch - 1])
                lnv.inc(ve.tensor_scalar(s_xa[:, :], s_x[ch % 2][:, :],
                                         s_stat[:, 0:1], s_stat[:, 1:2],
                                         ALU.add, ALU.mult), 1)
            block.vector(a1_ap)
            lna_done.append(lnv.n)

            def a1_st(sy, ch=ch):
                sy.wait_ge(lnv.h, lna_done[ch])
                dst.inc(sy.dma_start(out=xa_dram[ch * 128:(ch + 1) * 128, :], in_=s_xa[:, :]), 16)
            block.sync(a1_st)
            sta_done.append(dst.n)

        DST_A1 = dst.n
        TMM_PREV = tmm.n

        def a2(sy, DST_A1=DST_A1, b=b, TMM_PREV=TMM_PREV):
            sy.wait_ge(dst.h, DST_A1)
            if b == 1:
                sy.wait_ge(tmm.h, TMM_PREV)
            for c in range(8):
                trx.inc(sy.dma_start_transpose(XF[:, c, :], xa_dram[:, c * 128:(c + 1) * 128]), 16)
        block.sync(a2)
        TRX_A2 = trx.n

        EV0 = vst.n
        TM0 = tmm.n

        def a3_mm(te, TRX_A2=TRX_A2, EV0=EV0, GLD_W=GLD_W):
            te.wait_ge(trx.h, TRX_A2)
            te.wait_ge(gld.h, GLD_W)
            k = 0
            for wsb in (WDQ, WDKV):
                for latch in range(4):
                    for sc4 in range(4):
                        pt = p_a if (k % 2 == 0) else p_b
                        if k >= 2:
                            te.wait_ge(vst.h, EV0 + k - 1)
                        for dch in range(8):
                            mm_ = te.matmul(pt[:, :], wsb[:, dch, latch * 128:(latch + 1) * 128],
                                      XF[:, dch, sc4 * 512:(sc4 + 1) * 512],
                                      start=(dch == 0), stop=(dch == 7))
                        tmm.inc(mm_, 1)
                        k += 1
        block.tensor(a3_mm)

        def a3_ev(ve, TM0=TM0):
            k = 0
            for dv in (CQ, CKV):
                for latch in range(4):
                    for sc4 in range(4):
                        pt = p_a if (k % 2 == 0) else p_b
                        ve.wait_ge(tmm.h, TM0 + k + 1)
                        vst.inc(ve.tensor_copy(dv[:, latch, sc4 * 512:(sc4 + 1) * 512], pt[:, :]), 1)
                        k += 1
        block.vector(a3_ev)
        EV_CQ = vst.n

        TM_KR0 = tmm.n

        def a3_kr_mm(te, EV_CQ=EV_CQ):
            for k in range(4):
                te.wait_ge(vst.h, EV_CQ + k)
                for dch in range(8):
                    te.matmul(p_a[0:16, :], WKR[:, dch, 0:16],
                              XF[:, dch, k * 512:(k + 1) * 512],
                              start=(dch == 0), stop=(dch == 7))
                    mm_ = te.matmul(p_b[0:16, :], WKR[:, dch, 16:32],
                                    XF[:, dch, k * 512:(k + 1) * 512],
                                    start=(dch == 0), stop=(dch == 7))
                tmm.inc(mm_, 1)
        block.tensor(a3_kr_mm)
        TM_A3 = tmm.n

        def a3_kr_ev(ve, TM_KR0=TM_KR0):
            for k in range(4):
                ve.wait_ge(tmm.h, TM_KR0 + k + 1)
                ve.tensor_copy(s_krr1[:, k * 512:(k + 1) * 512], p_a[0:16, :])
                vst.inc(ve.tensor_copy(s_krr2[:, k * 512:(k + 1) * 512], p_b[0:16, :]), 1)
            co, si = s_cosq[0:16, :], s_sinq[0:16, :]
            ve.tensor_tensor(s_kr1[:, :], s_krr1, co, ALU.mult)
            ve.tensor_tensor(s_kr2[:, :], s_krr1, si, ALU.mult)
            ve.tensor_tensor(s_dump[0:16, 0:1024].bitcast(BF16), s_krr2, si, ALU.mult)
            ve.tensor_sub(s_kr1[:, :], s_kr1[:, :], s_dump[0:16, 0:1024].bitcast(BF16)[0:16, 0:S])
            ve.tensor_tensor(s_dump[0:16, 0:1024].bitcast(BF16), s_krr2, co, ALU.mult)
            vst.inc(ve.tensor_add(s_kr2[:, :], s_kr2[:, :],
                                  s_dump[0:16, 0:1024].bitcast(BF16)[0:16, 0:S]), 1)
        block.vector(a3_kr_ev)
        EV_KR = vst.n

        def wo_ld(gp, TM_A3=TM_A3):
            gp.wait_ge(tmm.h, TM_A3)
            gld.inc(gp.dma_start(
                out=WO, in_=wo_t.ap().rearrange("(c p) l -> p c l", p=128)), 16)
        block.gpsimd(wo_ld)
        GLD_WO = gld.n

        def vt_pre(ve, b=b):
            if b == 1:
                ve.wait_ge(spv.h, pair_last_tile[(0, 7)] + 1)
            ve.memset(s_vtp[0][:, :], 1.0)
            vst.inc(ve.memset(s_vtp[1][:, :], 1.0), 1)
        block.vector(vt_pre)

        qk_ready = {}
        qr_ready = {}
        upld = {}
        build_tm = {}

        def emit_up_load(p):
            def _g(gp, p=p, b=b):
                if (b, p - 2) in build_tm:
                    gp.wait_ge(tmm.h, build_tm[(b, p - 2)])
                h0 = 2 * p
                for dstv, srct in ((UPQ[p % 2], wuq_t), (UPK[p % 2], wuk_t),
                                   (UPV[p % 2], wuv_t)):
                    gld.inc(gp.dma_start(
                        out=dstv,
                        in_=srct.ap()[:, h0 * 64:h0 * 64 + 128].rearrange("(c p) l -> p c l", p=128)), 16)
                if p % 2 == 0:
                    q4 = p // 2
                    gld.inc(gp.dma_start(
                        out=UQR[q4 % 2],
                        in_=wqr_t.ap()[:, q4 * 128:(q4 + 1) * 128].rearrange("(c p) l -> p c l", p=128)), 16)
            block.gpsimd(_g)
            upld[p] = gld.n

        def emit_build(p):
            h0, h1 = 2 * p, 2 * p + 1
            q4 = p // 2
            bufA, bufB = 2 * (p % 2), 2 * (p % 2) + 1
            GLDT = upld[p]

            if p % 2 == 0:
                EVb = vst.n

                def qr_mm(te, q4=q4, GLDT=GLDT, EVb=EVb):
                    te.wait_ge(gld.h, GLDT)
                    for k in range(4):
                        pt = p_a if (k % 2 == 0) else p_b
                        te.wait_ge(vst.h, EVb + max(0, k - 1))
                        for latch in range(4):
                            mm_ = te.matmul(pt[:, :], UQR[q4 % 2][:, latch, :],
                                      CQ[:, latch, k * 512:(k + 1) * 512],
                                      start=(latch == 0), stop=(latch == 3))
                        tmm.inc(mm_, 1)
                block.tensor(qr_mm)
                TMq = tmm.n

                def qr_ev(ve, q4=q4, TMq=TMq):
                    for k in range(4):
                        pt = p_a if (k % 2 == 0) else p_b
                        ve.wait_ge(tmm.h, TMq - 4 + k + 1)
                        vst.inc(ve.tensor_copy(s_qr4[:, k * 512:(k + 1) * 512], pt[:, :]), 1)
                    dstb = s_qr4r[0]
                    ve.tensor_tensor(dstb[0:64, :], s_qr4[0:64, :], s_cosq[0:64, :], ALU.mult)
                    ve.tensor_tensor(dstb[64:128, :], s_qr4[0:64, :], s_sinq[0:64, :], ALU.mult)
                    ve.tensor_tensor(s_qr4[0:64, :], s_qr4[64:128, :], s_sinq[64:128, :], ALU.mult)
                    ve.tensor_tensor(s_qr4[64:128, :], s_qr4[64:128, :], s_cosq[64:128, :], ALU.mult)
                    ve.tensor_sub(dstb[0:64, :], dstb[0:64, :], s_qr4[0:64, :])
                    vst.inc(ve.tensor_add(dstb[64:128, :], dstb[64:128, :], s_qr4[64:128, :]), 1)
                block.vector(qr_ev)
                qr_ready[(b, q4)] = vst.n

            EVv = vst.n

            def v_mm(te, p=p, GLDT=GLDT, EVv=EVv):
                te.wait_ge(gld.h, GLDT)
                for sch in range(SCH):
                    pt = p_a if (sch % 2 == 0) else p_b
                    te.wait_ge(vst.h, EVv + max(0, sch - 1))
                    for latch in range(4):
                        mm_ = te.matmul(pt[:, 0:128], CKV[:, latch, sch * 128:(sch + 1) * 128],
                                  UPV[p % 2][:, latch, :],
                                  start=(latch == 0), stop=(latch == 3))
                    tmm.inc(mm_, 1)
            block.tensor(v_mm)
            TMv = tmm.n

            def v_ev(ve, p=p, b=b, TMv=TMv):
                if (b, p - 2) in pair_last_tile:
                    ve.wait_ge(spv.h, pair_last_tile[(b, p - 2)] + 1)
                for sch in range(SCH):
                    pt = p_a if (sch % 2 == 0) else p_b
                    ve.wait_ge(tmm.h, TMv - SCH + sch + 1)
                    dstap = VTP[p % 2][:, sch, 0:130].rearrange("p (h w) -> p h w", w=65)[:, :, 0:64]
                    vst.inc(ve.tensor_copy(dstap,
                                           pt[:, 0:128].rearrange("p (h w) -> p h w", w=64)), 1)
            block.vector(v_ev)

            EVb2 = vst.n

            def qk_mm(te, p=p, GLDT=GLDT, EVb2=EVb2):
                te.wait_ge(gld.h, GLDT)
                k = 0
                for wview, cview in ((UPQ[p % 2], CQ), (UPK[p % 2], CKV)):
                    for sc4 in range(4):
                        pt = p_a if (k % 2 == 0) else p_b
                        te.wait_ge(vst.h, EVb2 + max(0, k - 1))
                        for latch in range(4):
                            mm_ = te.matmul(pt[:, :], wview[:, latch, :],
                                      cview[:, latch, sc4 * 512:(sc4 + 1) * 512],
                                      start=(latch == 0), stop=(latch == 3))
                        tmm.inc(mm_, 1)
                        k += 1
            block.tensor(qk_mm)
            build_tm[(b, p)] = tmm.n
            TMb = tmm.n

            def qk_ev(ve, p=p, b=b, TMb=TMb, bufA=bufA, bufB=bufB, q4=q4):
                if (b, p - 2) in pair_last_tile:
                    ve.wait_ge(smm.h, pair_last_tile[(b, p - 2)] + 1)
                k = 0
                for dsts in ((s_qh[bufA], s_qh[bufB]), (s_kh[bufA], s_kh[bufB])):
                    for sc4 in range(4):
                        pt = p_a if (k % 2 == 0) else p_b
                        ve.wait_ge(tmm.h, TMb - 8 + k + 1)
                        ve.tensor_copy(dsts[0][0:64, sc4 * 512:(sc4 + 1) * 512], pt[0:64, :])
                        vst.inc(ve.tensor_copy(dsts[1][0:64, sc4 * 512:(sc4 + 1) * 512],
                                               pt[64:128, :]), 1)
                        k += 1
            block.vector(qk_ev)
            QK_EVN = vst.n

            def qk_cp(sy, p=p, b=b, QK_EVN=QK_EVN, bufA=bufA, bufB=bufB, q4=q4,
                      TRX_A2=TRX_A2):
                sy.wait_ge(trx.h, TRX_A2)
                sy.wait_ge(vst.h, max(QK_EVN, qr_ready[(b, q4)], EV_KR))
                if (b, p - 2) in pair_last_tile:
                    sy.wait_ge(smm.h, pair_last_tile[(b, p - 2)] + 1)
                rope = s_qr4r[0]
                for hh, qb, kb in ((2 * p, s_qh[bufA], s_kh[bufA]),
                                   (2 * p + 1, s_qh[bufB], s_kh[bufB])):
                    hl = hh % 4
                    qcp.inc(sy.dma_start(out=qb[64:80, :], in_=rope[16 * hl:16 * hl + 16, :]), 16)
                    qcp.inc(sy.dma_start(out=qb[80:96, :], in_=rope[64 + 16 * hl:80 + 16 * hl, :]), 16)
                    qcp.inc(sy.dma_start(out=kb[64:80, :], in_=s_kr1[:, :]), 16)
                    qcp.inc(sy.dma_start(out=kb[80:96, :], in_=s_kr2[:, :]), 16)
            block.sync(qk_cp)
            qk_ready[(b, p)] = (vst.n, qcp.n)

        def emit_flash_head(h, buf, vbuf):
            qb, kb = s_qh[buf], s_kh[buf]
            hp = h % 2
            for j in range(4):
                aj = AIDX[(b, h, j)]
                nkt = 4 * j + 4
                t0 = TIDX[(b, h, j, 0)]
                READY = qk_ready[(b, h // 2)]

                def fl_t(te, h=h, j=j, aj=aj, nkt=nkt, t0=t0, qb=qb, kb=kb,
                         READY=READY, vbuf=vbuf, hp=hp):
                    te.wait_ge(vst.h, READY[0])
                    te.wait_ge(qcp.h, READY[1])
                    if aj >= 2:
                        te.wait_ge(satv.h, aj - 1)
                    pend = []

                    def do_pv(tt):
                        kt_ = tt - t0
                        te.wait_ge(sclp.h, tt + 1)
                        pat = p_at0 if (j % 2 == 0) else p_at1
                        spv.inc(te.matmul(pat[:, :],
                                          VTP[vbuf][:, kt_, hp * 65:hp * 65 + 65],
                                          s_p[tt % 3][:, :],
                                          start=(kt_ == 0), stop=(kt_ == nkt - 1)), 1)

                    for kt in range(nkt):
                        tt = t0 + kt
                        if tt >= 2:
                            te.wait_ge(sexp.h, tt - 1)
                        pt = p_s0 if (tt % 2 == 0) else p_s1
                        smm.inc(te.matmul(pt[:, :], kb[:, kt * 128:(kt + 1) * 128],
                                          qb[:, j * 512:(j + 1) * 512],
                                          start=True, stop=True), 1)
                        pend.append(tt)
                        if len(pend) >= 3:
                            do_pv(pend.pop(0))
                    while pend:
                        do_pv(pend.pop(0))
                    te.wait_ge(sden.h, aj + 1)
                    if aj >= 1:
                        te.wait_ge(sbc.h, aj)
                    sbouter.inc(te.matmul(p_bc[:, :], s_ones[:, :], s_rrow[:, :],
                                          start=True, stop=True), 1)
                block.tensor(fl_t)

                def fl_s(sc, t0=t0, nkt=nkt):
                    for kt in range(nkt):
                        tt = t0 + kt
                        sc.wait_ge(smm.h, tt + 1)
                        if tt >= 3:
                            sc.wait_ge(sclp.h, tt - 2)
                        pt = p_s0 if (tt % 2 == 0) else p_s1
                        sexp.inc(sc.activation(s_praw[tt % 3][:, :], pt[:, :],
                                               AF.Exp, scale=ALPHA), 1)
                block.scalar(fl_s)

                def fl_v(ve, t0=t0, nkt=nkt, j=j, aj=aj, h=h):
                    for kt in range(nkt):
                        tt = t0 + kt
                        ve.wait_ge(sexp.h, tt + 1)
                        if tt >= 3:
                            ve.wait_ge(spv.h, tt - 2)
                        ve.tensor_scalar(s_p[tt % 3][:, :], s_praw[tt % 3][:, :],
                                         CLO, CHI, ALU.max, ALU.min)
                        dk = kt - 4 * j
                        if dk >= 0:
                            pre = 128 * dk
                            if pre > 0:
                                ve.memset(s_p[tt % 3][:, 0:pre], 0.0)
                            sclp.inc(ve.tensor_tensor(s_p[tt % 3][:, pre:pre + 128],
                                                      s_p[tt % 3][:, pre:pre + 128],
                                                      s_tri[:, :], ALU.mult), 1)
                        else:
                            sclp.inc(ve.engine_nop(), 1)
                    ve.wait_ge(spv.h, aj_last_tile[(b, h, j)] + 1)
                    if aj >= 1:
                        ve.wait_ge(sbouter.h, aj)
                    pat = p_at0 if (j % 2 == 0) else p_at1
                    sden.inc(ve.reciprocal(s_rrow[:, :], pat[64:65, :]), 1)
                block.vector(fl_v)

                def fl_s2(sc, aj=aj):
                    sc.wait_ge(sbouter.h, aj + 1)
                    if aj >= 2:
                        sc.wait_ge(satv.h, aj - 1)
                    sbc.inc(sc.activation(s_rb[aj % 2][:, :], p_bc[0:64, :], AF.Copy), 1)
                block.scalar(fl_s2)

                def fl_v2(ve, aj=aj, h=h, j=j):
                    ve.wait_ge(sbc.h, aj + 1)
                    pat = p_at0 if (j % 2 == 0) else p_at1
                    och, po = h // 2, (h % 2) * 64
                    satv.inc(ve.tensor_tensor(AFm[po:po + 64, och, j * 512:(j + 1) * 512],
                                              pat[0:64, :], s_rb[aj % 2][:, :], ALU.mult), 1)
                block.vector(fl_v2)

        emit_up_load(0)
        emit_build(0)
        for p in range(H // 2):
            if p + 1 < H // 2:
                emit_up_load(p + 1)
                emit_build(p + 1)
            emit_flash_head(2 * p, 2 * (p % 2), p % 2)
            emit_flash_head(2 * p + 1, 2 * (p % 2) + 1, p % 2)

        SATV_ALL = satv.n
        a5_apply = []
        a5_xn = []
        a5_store = []
        for ch in range(SCH):
            tch = b * SCH + ch

            def a5_ld(sy, ch=ch, b=b):
                if ch >= 2:
                    sy.wait_ge(lnv.h, a5_xn[ch - 2])
                ldx.inc(sy.dma_start(out=s_x[ch % 2][:, :],
                                     in_=x_in[b * S + ch * 128:b * S + (ch + 1) * 128, :]), 16)
            block.sync(a5_ld)
            LDX_T = ldx.n

            def a5_mm(te, ch=ch, SATV_ALL=SATV_ALL, GLD_WO=GLD_WO):
                if ch == 0:
                    te.wait_ge(satv.h, SATV_ALL)
                    te.wait_ge(gld.h, GLD_WO)
                else:
                    te.wait_ge(vst.h, a5_apply[ch - 1])
                for dh in range(2):
                    pt = p_a if dh == 0 else p_b
                    for och in range(8):
                        mm_ = te.matmul(pt[:, :], AFm[:, och, ch * 128:(ch + 1) * 128],
                                  WO[:, och, dh * 512:(dh + 1) * 512],
                                  start=(och == 0), stop=(och == 7))
                tmm.inc(mm_, 1)
            block.tensor(a5_mm)
            TM_T = tmm.n

            emit_ln([p_a[:, :], p_b[:, :]],
                    pre_wait_v=lambda ve, t=TM_T: ve.wait_ge(tmm.h, t),
                    pre_wait_s=lambda sc, t=TM_T: sc.wait_ge(tmm.h, t))

            def a5_v(ve, ch=ch, LDX_T=LDX_T):
                ve.tensor_scalar(s_ao[:, 0:512], p_a[:, :], s_stat[:, 0:1], s_stat[:, 1:2],
                                 ALU.add, ALU.mult)
                vst.inc(ve.tensor_scalar(s_ao[:, 512:1024], p_b[:, :], s_stat[:, 0:1],
                                         s_stat[:, 1:2], ALU.add, ALU.mult), 1)
                ve.wait_ge(ldx.h, LDX_T)
                if ch >= 1:
                    ve.wait_ge(dst.h, a5_store[ch - 1])
                vst.inc(ve.tensor_add(s_x2c[:, :], s_x[ch % 2][:, :], s_ao[:, :]), 1)
            block.vector(a5_v)
            a5_apply.append(vst.n)
            VST_X2 = vst.n

            def a5_st1(sy, tch=tch, VST_X2=VST_X2):
                sy.wait_ge(vst.h, VST_X2)
                dst.inc(sy.dma_start(out=x2_dram[tch * 128:(tch + 1) * 128, :], in_=s_x2c[:, :]), 16)
            block.sync(a5_st1)

            emit_ln([s_x2c[:, :]],
                    pre_wait_v=lambda ve, t=VST_X2: ve.wait_ge(vst.h, t),
                    pre_wait_s=lambda sc, t=VST_X2: sc.wait_ge(vst.h, t))

            def a5_v2(ve, ch=ch):
                if ch >= 1:
                    ve.wait_ge(dst.h, a5_store[ch - 1])
                lnv.inc(ve.tensor_scalar(s_xnc[:, :], s_x2c[:, :], s_stat[:, 0:1],
                                         s_stat[:, 1:2], ALU.add, ALU.mult), 1)
            block.vector(a5_v2)
            a5_xn.append(lnv.n)
            LNV_XN = lnv.n

            def a5_st2(sy, tch=tch, LNV_XN=LNV_XN):
                sy.wait_ge(lnv.h, LNV_XN)
                dst.inc(sy.dma_start(out=xnw_own[tch * 128:(tch + 1) * 128, 0:D], in_=s_xnc[:, :]), 16)
            block.sync(a5_st2)
            a5_store.append(dst.n)

    if stages <= 1:
        return _quiesce()
    barrier()
    attn_ctx.close()

    hact_ctx = ExitStack()
    s_hact = hact_ctx.enter_context(nc.sbuf_tensor("s_hact", [128, 8 * 512], BF16))
    HA = s_hact[:, :].rearrange("p (c l) -> p c l", l=512)
    s_hsil = hact_ctx.enter_context(nc.sbuf_tensor("s_hsil", [128, 512], BF16))

    bz_ctx = ExitStack()
    sbb = lambda n_, s_, d_: bz_ctx.enter_context(nc.sbuf_tensor(n_, s_, d_))

    s_xnfm = sbb("s_xnfm", [128, 8 * TPC], BF16)
    XNF = s_xnfm[:, :].rearrange("p (c t) -> p c t", t=TPC)
    s_wg = sbb("s_wg", [128, 8 * E], BF16)
    WGv = s_wg[:, :].rearrange("p (c e) -> p c e", e=E)
    s_g = sbb("s_g", [128, 64], F32)
    s_g2 = sbb("s_g2", [128, 32], F32)
    s_wsacc = sbb("s_wsacc", [128, NCH * E], F32)
    WSA = s_wsacc[:, :].rearrange("p (c e) -> p c e", e=E)
    s_sw1 = sbb("s_sw1", [128, 8 * INTER], BF16)
    s_sw3 = sbb("s_sw3", [128, 8 * INTER], BF16)
    s_sw2 = sbb("s_sw2", [128, 8 * D], BF16)
    SW1 = s_sw1[:, :].rearrange("p (c l) -> p c l", l=INTER)
    SW3 = s_sw3[:, :].rearrange("p (c l) -> p c l", l=INTER)
    SW2 = s_sw2[:, :].rearrange("p (c l) -> p c l", l=D)
    s_zc = sbb("s_zc", [128, D], BF16)

    DST_A = dst.n

    def b_trx(sy, DST_A=DST_A):
        sy.wait_ge(dst.h, DST_A)
        for c in range(8):
            trx.inc(sy.dma_start_transpose(XNF[:, c, :],
                                           xnw_own.ap()[:, c * 128:(c + 1) * 128]), 16)
    block.sync(b_trx)
    TRX_B = trx.n

    def b_ldw(gp):
        gld.inc(gp.dma_start(out=WGv, in_=wg_t.ap().rearrange("(c p) e -> p c e", p=128)), 16)
        for dstv, srct in ((SW1, sw1_t), (SW3, sw3_t), (SW2, sw2_t)):
            gld.inc(gp.dma_start(out=dstv, in_=srct.ap().rearrange("(c p) l -> p c l", p=128)), 16)
    block.gpsimd(b_ldw)
    GLD_B = gld.n

    bg_done = []
    bg_exp = []
    for ch in range(NCH):
        def bg_mm(te, ch=ch):
            if ch == 0:
                te.wait_ge(trx.h, TRX_B)
                te.wait_ge(gld.h, GLD_B)
            if ch >= 2:
                te.wait_ge(vst.h, bg_done[ch - 2])
            pt = p_a if (ch % 2 == 0) else p_b
            for dch in range(8):
                mm_ = te.matmul(pt[0:128, 0:E], XNF[:, dch, ch * 128:(ch + 1) * 128],
                          WGv[:, dch, :], start=(dch == 0), stop=(dch == 7))
            tmm.inc(mm_, 1)
        block.tensor(bg_mm)
        TMG = tmm.n

        def bg_v1(ve, ch=ch, TMG=TMG):
            ve.wait_ge(tmm.h, TMG)
            if ch >= 1:
                ve.wait_ge(sst.h, bg_exp[ch - 1])
            pt = p_a if (ch % 2 == 0) else p_b
            vst.inc(ve.reduce_max(s_stat[:, 10:11], pt[0:128, 0:E], axis=AX.X, negate=True), 1)
        block.vector(bg_v1)
        VPRE = vst.n

        def bg_sc(sc, ch=ch, VPRE=VPRE):
            sc.wait_ge(vst.h, VPRE)
            pt = p_a if (ch % 2 == 0) else p_b
            sst.inc(sc.activation(s_g[:, 0:E], pt[0:128, 0:E], AF.Exp,
                                  bias=s_stat[:, 10:11], accum_out=s_stat[:, 11:12]), 1)
        block.scalar(bg_sc)
        bg_exp.append(sst.n)
        SD = sst.n

        def bg_v2(ve, ch=ch, SD=SD):
            ve.wait_ge(sst.h, SD)
            ve.reciprocal(s_stat[:, 10:11], s_stat[:, 11:12])
            ve.tensor_scalar(s_g[:, 0:E], s_g[:, 0:E], s_stat[:, 10:11], None, ALU.mult)
            ve.tensor_reduce(s_g[:, 32:40], s_g[:, 0:E].rearrange("p (g f) -> p g f", f=4),
                             axis=AX.X, op=ALU.max)
            ve.max(out=s_g[:, 40:48], in_=s_g[:, 32:40])
            ve.tensor_scalar(s_g[:, 48:56], s_g[:, 32:40], s_g[:, 43:44], None, ALU.is_ge)
            ve.tensor_copy(s_g2[:, 0:E].rearrange("p (g f) -> p g f", f=4),
                           s_g[:, 48:56].rearrange("p (g o) -> p g o", o=1).to_broadcast([128, 8, 4]))
            ve.scalar_tensor_tensor(s_g2[:, 0:E], s_g[:, 0:E], 2.0, s_g2[:, 0:E],
                                    ALU.add, ALU.mult)
            ve.max(out=s_g[:, 40:48], in_=s_g2[:, 0:E])
            ve.tensor_scalar(s_g2[:, 0:E], s_g2[:, 0:E], s_g[:, 43:44], None, ALU.is_ge)
            ve.tensor_tensor(s_g[:, 0:E], s_g[:, 0:E], s_g2[:, 0:E], ALU.mult)
            ve.tensor_reduce(s_stat[:, 11:12], s_g[:, 0:E], axis=AX.X, op=ALU.add)
            ve.tensor_scalar(s_stat[:, 11:12], s_stat[:, 11:12], 1e-9, None, ALU.add)
            ve.reciprocal(s_stat[:, 10:11], s_stat[:, 11:12])
            vst.inc(ve.tensor_scalar(WSA[:, ch, :], s_g[:, 0:E], s_stat[:, 10:11],
                                     None, ALU.mult), 1)
        block.vector(bg_v2)
        bg_done.append(vst.n)
    VST_GATE = vst.n

    def b_store(sy):
        for ch in range(NCH):
            sy.wait_ge(vst.h, bg_done[ch])
            dst.inc(sy.dma_start(out=wsel_own[ch * 128:(ch + 1) * 128, :],
                                 in_=WSA[:, ch, :]), 16)
    block.sync(b_store)

    def b_store2(gp):
        for ch in range(NCH):
            gp.wait_ge(vst.h, bg_done[ch])
            wsb.inc(gp.dma_start(
                out=xnw_own[ch * 128:(ch + 1) * 128, D:ROWW],
                in_=WSA[:, ch, :]), 16)
    block.gpsimd(b_store2)
    DST_B = dst.n
    WSB_B = wsb.n

    ZTOT = (NTOK + 1) * D
    NZFULL = ZTOT // (128 * 1024)
    ZREM = ZTOT - NZFULL * 128 * 1024

    def b_zero(gp):
        gp.wait_ge(vst.h, VST_ZMEM)
        flat = y_acc.ap().rearrange("a b -> (a b)")
        for r in range(NZFULL):
            zro.inc(gp.dma_start(
                out=flat[r * 128 * 1024:(r + 1) * 128 * 1024].rearrange("(p c) -> p c", p=128),
                in_=s_zero[:, :]), 16)
        if ZREM:
            zro.inc(gp.dma_start(
                out=flat[NZFULL * 128 * 1024:].rearrange("(p c) -> p c", p=1),
                in_=s_zero[0:1, 0:ZREM]), 16)
        zro.inc(gp.dma_start(out=xnw_all[NTOK:NTOK + 1, 0:1024], in_=s_zero[0:1, 0:1024]), 16)
        zro.inc(gp.dma_start(out=xnw_all[NTOK:NTOK + 1, 1024:ROWW], in_=s_zero[0:1, 0:WW]), 16)
    block.gpsimd(b_zero)

    def b_ag(gp, DST_B=DST_B):
        gp.wait_ge(dst.h, DST_B)
        gp.wait_ge(wsb.h, WSB_B)
        ccs.inc(gp.collective_compute(
            "AllGather", ALU.bypass, replica_groups=rg,
            ins=[xnw_own.ap().opt()], outs=[xnw_all.ap()[0:NTOK, :].opt()]), 1)
        ccs.inc(gp.collective_compute(
            "AllGather", ALU.bypass, replica_groups=rg,
            ins=[wsel_own.ap().opt()], outs=[wsel_all.ap().opt()]), 1)
    block.gpsimd(b_ag)
    CC_AG = ccs.n

    if stages <= 2:
        return _quiesce()
    z_hcons = [VST_GATE]
    z_yst = []
    for g in range(TPC // 512):
        for ich in range(8):
            def z_hmm(te, g=g, ich=ich):
                k = g * 8 + ich
                te.wait_ge(vst.h, z_hcons[-1] if k == 0 else z_hcons[k])
                for dch in range(8):
                    mm_ = te.matmul(p_a[:, :], SW1[:, dch, ich * 128:(ich + 1) * 128],
                              XNF[:, dch, g * 512:(g + 1) * 512],
                              start=(dch == 0), stop=(dch == 7))
                tmm.inc(mm_, 1)
                for dch in range(8):
                    mm_ = te.matmul(p_b[:, :], SW3[:, dch, ich * 128:(ich + 1) * 128],
                              XNF[:, dch, g * 512:(g + 1) * 512],
                              start=(dch == 0), stop=(dch == 7))
                tmm.inc(mm_, 1)
            block.tensor(z_hmm)
            TMH = tmm.n

            def z_sil(sc, TMH=TMH):
                sc.wait_ge(tmm.h, TMH - 1)
                sst.inc(sc.activation(s_hsil[:, :], p_a[:, :], AF.Silu), 1)
            block.scalar(z_sil)
            SSH = sst.n

            def z_mul(ve, ich=ich, TMH=TMH, SSH=SSH):
                ve.wait_ge(tmm.h, TMH)
                ve.wait_ge(sst.h, SSH)
                vst.inc(ve.tensor_tensor(HA[:, ich, :], p_b[:, :], s_hsil[:, :], ALU.mult), 1)
            block.vector(z_mul)
            z_hcons.append(vst.n)
        for sl in range(4):
            gs = g * 4 + sl

            def z_ymm(te, g=g, sl=sl, gs=gs):
                te.wait_ge(vst.h, z_hcons[g * 8 + 8])
                if gs >= 1:
                    te.wait_ge(vst.h, z_yev_l[gs - 1])
                for dh in range(2):
                    pt = p_s0 if dh == 0 else p_s1
                    for ich in range(8):
                        mm_ = te.matmul(pt[:, :], HA[:, ich, sl * 128:(sl + 1) * 128],
                                  SW2[:, ich, dh * 512:(dh + 1) * 512],
                                  start=(ich == 0), stop=(ich == 7))
                tmm.inc(mm_, 1)
            if gs == 0:
                z_yev_l = []
            block.tensor(z_ymm)
            TMZ = tmm.n

            def z_yev(ve, TMZ=TMZ, gs=gs):
                ve.wait_ge(tmm.h, TMZ)
                if gs >= 1:
                    ve.wait_ge(dst.h, z_yst[gs - 1])
                ve.tensor_copy(s_zc[:, 0:512], p_s0[:, :])
                vst.inc(ve.tensor_copy(s_zc[:, 512:1024], p_s1[:, :]), 1)
            block.vector(z_yev)
            z_yev_l.append(vst.n)
            VSTZ = vst.n

            def z_st(sy, gs=gs, VSTZ=VSTZ):
                sy.wait_ge(vst.h, VSTZ)
                dst.inc(sy.dma_start(out=z_dram[gs * 128:(gs + 1) * 128, :], in_=s_zc[:, :]), 16)
            block.sync(z_st)
            z_yst.append(dst.n)
    TM_Z = tmm.n
    VST_Z = vst.n
    DST_Z = dst.n

    if stages <= 3:
        return _quiesce()
    barrier()
    bz_ctx.close()

    cd_ctx = ExitStack()
    sbc_ = lambda n_, s_, d_: cd_ctx.enter_context(nc.sbuf_tensor(n_, s_, d_))
    s_maskv = sbc_("s_maskv", [128, EPC * CAPJ], F32)
    MKV = s_maskv[:, :].rearrange("p (i j) -> p i j", j=CAPJ)
    s_idx32 = sbc_("s_idx32", [128, EPC * CAPJ], I32)
    IDXv = s_idx32[:, :].rearrange("p (i j) -> p i j", j=CAPJ)
    s_bid32 = sbc_("s_bid32", [128, EPC * CAPJ], I32)
    BID32 = s_bid32[:, :].rearrange("p (i j) -> p i j", j=CAPJ)
    s_bufw = sbc_("s_bufw", [128, EPC * CAPJ], F32)
    BW = s_bufw[:, :].rearrange("p (i j) -> p i j", j=CAPJ)
    s_esel = sbc_("s_esel", [128, EPC * E], F32)
    ESL = s_esel[:, :].rearrange("p (i e) -> p i e", e=E)
    s_wtmp = sbc_("s_wtmp", [128, E], F32)
    s_wred = sbc_("s_wred", [128, 1], F32)
    s_rel = sbc_("s_rel", [128, EPC * CAPJ], F32)
    RELV = s_rel[:, :].rearrange("p (i j) -> p i j", j=CAPJ)
    s_blockid = sbc_("s_blockid", [128, EPC * CAPJ], F32)
    BID = s_blockid[:, :].rearrange("p (i j) -> p i j", j=CAPJ)
    s_irow = sbc_("s_irow", [128, 256], F32)

    c_ctx = ExitStack()
    sbcc = lambda n_, s_, d_: c_ctx.enter_context(nc.sbuf_tensor(n_, s_, d_))
    s_wsel = sbcc("s_wsel", [128, 256 * E], F32)
    WSL = s_wsel[:, :].rearrange("p (c e) -> p c e", e=E)
    s_msk = sbcc("s_msk", [128, 256 * E], F32)
    MSKv = s_msk[:, :].rearrange("p (c e) -> p c e", e=E)
    s_vB = sbcc("s_vB", [128, 256], F32)
    s_selB = sbcc("s_selB", [128, 256], F32)
    s_scanA = sbcc("s_scanA", [128, 256], F32)
    s_scanB = sbcc("s_scanB", [128, 256], F32)
    s_rinf = sbcc("s_rinf", [128, 256], F32)
    s_cntcol = sbcc("s_cntcol", [128, 1], F32)
    s_cntrow = sbcc("s_cntrow", [1, 128], F32)
    s_cb1 = sbcc("s_cb1", [1, 128], F32)
    s_cb2 = sbcc("s_cb2", [1, 128], F32)
    s_exclb = sbcc("s_exclb", [128, 128], F32)
    s_bidxb = sbcc("s_bidxb", [128, 128], F32)
    s_cntf = sbcc("s_cntf", [1, 1], F32)
    s_cnt128 = sbcc("s_cnt128", [128, EPC], F32)
    s_base = sbcc("s_base", [128, CAPJ], F32)
    s_slotio = sbcc("s_slotio", [128, CAPJ], F32)
    s_identf = sbcc("s_identf", [128, 128], F32)
    s_brow128 = sbcc("s_brow128", [1, 128], F32)

    rankT = [dram(f"rankT{i}", [128, 256], F32) for i in range(EPC)]

    def c_ld(sy):
        ldx.inc(sy.dma_start(out=s_slotio[:, :], in_=slotiota[:, :]), 16)
        ldx.inc(sy.dma_start(out=s_esel[:, :], in_=esel_t[:, :]), 16)
        ldx.inc(sy.dma_start(out=s_identf[:, :], in_=identf_t[:, :]), 16)
        ldx.inc(sy.dma_start(out=s_irow[:, :], in_=irow_t[:, :]), 16)
        ldx.inc(sy.dma_start(out=s_brow128[:, :], in_=brow128_t[:, :]), 16)
        sy.wait_ge(ccs.h, CC_AG)
        ldx.inc(sy.dma_start(out=s_wsel[:, :],
                             in_=wsel_all.ap().rearrange("(p t) e -> p (t e)", p=128)), 16)
    block.sync(c_ld)
    LDX_C = ldx.n

    c_mm_done = []
    c_rdy = []
    c_rank_st = []
    for i in range(EPC):
        def c_v1(ve, i=i, LDX_C=LDX_C):
            if i == 0:
                ve.wait_ge(ldx.h, LDX_C)
                ve.wait_ge(tmm.h, TM_Z)
            if i >= 1:
                ve.wait_ge(tmm.h, c_mm_done[i - 1])
            ve.tensor_tensor(MSKv, WSL,
                             ESL[:, i:i + 1, :].to_broadcast([128, 256, E]), ALU.mult)
            ve.tensor_reduce(s_vB[:, :], MSKv, axis=AX.X, op=ALU.add)
            ve.tensor_scalar(s_selB[:, :], s_vB[:, :], 0.0, None, ALU.is_gt)
            cur, oth = s_scanA, s_scanB
            ve.tensor_copy(cur[:, :], s_selB[:, :])
            sh = 1
            while sh < 256:
                ve.tensor_copy(oth[:, 0:sh], cur[:, 0:sh])
                ve.tensor_tensor(oth[:, sh:256], cur[:, sh:256], cur[:, 0:256 - sh], ALU.add)
                cur, oth = oth, cur
                sh *= 2
            ve.tensor_sub(s_rinf[:, :], s_scanA[:, :], s_selB[:, :])
            ve.tensor_scalar(s_selB[:, :], s_selB[:, :], -1048576.0, 1048576.0,
                             ALU.mult, ALU.add)
            ve.tensor_add(s_rinf[:, :], s_rinf[:, :], s_selB[:, :])
            vst.inc(ve.tensor_copy(s_cntcol[:, :], s_scanA[:, 255:256]), 1)
        block.vector(c_v1)
        VC1 = vst.n

        def c_st_rank(sy, i=i, VC1=VC1):
            sy.wait_ge(vst.h, VC1)
            dst.inc(sy.dma_start(out=rankT[i][:, :], in_=s_rinf[:, :]), 16)
        block.sync(c_st_rank)
        c_rank_st.append(dst.n)

        def c_mm2(te, VC1=VC1):
            te.wait_ge(vst.h, VC1)
            tmm.inc(te.matmul(p_a[0:1, 0:128], s_cntcol[:, :], s_identf[:, :],
                              start=True, stop=True), 1)
        block.tensor(c_mm2)
        TMC2 = tmm.n

        def c_v3(ve, i=i, TMC2=TMC2):
            ve.wait_ge(tmm.h, TMC2)
            ve.tensor_copy(s_cntrow[:, :], p_a[0:1, 0:128])
            cur, oth = s_cb1, s_cb2
            ve.tensor_copy(cur[:, :], s_cntrow[:, :])
            sh = 1
            while sh < 128:
                ve.tensor_copy(oth[:, 0:sh], cur[:, 0:sh])
                ve.tensor_tensor(oth[:, sh:128], cur[:, sh:128], cur[:, 0:128 - sh], ALU.add)
                cur, oth = oth, cur
                sh *= 2
            ve.tensor_sub(s_cb1[:, :], s_cb2[:, :], s_cntrow[:, :])
            ve.tensor_copy(s_cntf[:, :], s_cb2[:, 127:128])
            vst.inc(ve.tensor_scalar(s_cntf[:, :], s_cntf[:, :], float(CAP), None, ALU.min), 1)
        block.vector(c_v3)
        VC3 = vst.n

        def c_mm3(te, VC3=VC3):
            te.wait_ge(vst.h, VC3)
            tmm.inc(te.matmul(p_a[:, 0:128], s_ones[:, :], s_cb1[:, :],
                              start=True, stop=True), 1)
            tmm.inc(te.matmul(p_s0[:, 0:128], s_ones[:, :], s_brow128[:, :],
                              start=True, stop=True), 1)
            tmm.inc(te.matmul(p_b[:, 0:1], s_ones[:, :], s_cntf[:, :],
                              start=True, stop=True), 1)
        block.tensor(c_mm3)
        c_mm_done.append(tmm.n)
        TMC3 = tmm.n

        def c_v4(ve, i=i, TMC3=TMC3):
            ve.wait_ge(tmm.h, TMC3)
            ve.tensor_copy(s_exclb[:, :], p_a[:, 0:128])
            ve.tensor_copy(s_bidxb[:, :], p_s0[:, 0:128])
            ve.tensor_copy(s_cnt128[:, i:i + 1], p_b[:, 0:1])
            ve.tensor_scalar(MKV[:, i, :], s_slotio[:, :], s_cnt128[:, i:i + 1], None, ALU.is_lt)
            for j in range(CAPJ):
                ve.scalar_tensor_tensor(s_dump[:, 0:128], s_exclb[:, :], s_slotio[:, j:j + 1],
                                        s_exclb[:, :], ALU.is_le, ALU.mult)
                ve.tensor_reduce(s_base[:, j:j + 1], s_dump[:, 0:128], axis=AX.X, op=ALU.max)
                ve.scalar_tensor_tensor(s_dump[:, 0:128], s_exclb[:, :], s_slotio[:, j:j + 1],
                                        s_bidxb[:, :], ALU.is_le, ALU.mult)
                ve.tensor_reduce(BID[:, i, j:j + 1], s_dump[:, 0:128], axis=AX.X, op=ALU.max)
            vst.inc(ve.tensor_sub(RELV[:, i, :], s_slotio[:, :], s_base[:, :]), 1)
        block.vector(c_v4)
        c_rdy.append(vst.n)
    VST_C = vst.n

    if stages <= 4:
        return _quiesce()
    barrier()
    c_ctx.close()

    d_ctx = ExitStack()
    sbd = lambda n_, s_, d_: d_ctx.enter_context(nc.sbuf_tensor(n_, s_, d_))
    s_gbuf = [sbd("s_gbuf0", [128, ROWW], BF16), sbd("s_gbuf1", [128, ROWW], BF16)]
    s_grank = [sbd("s_grank0", [128, 256], F32), sbd("s_grank1", [128, 256], F32)]
    s_xefm = sbd("s_xefm", [128, 8 * CAP], BF16)
    XE = s_xefm[:, :].rearrange("p (c t) -> p c t", t=CAP)
    s_w1 = sbd("s_w1", [128, 8 * INTER], BF16)
    s_w3 = sbd("s_w3", [128, 8 * INTER], BF16)
    s_w2 = sbd("s_w2", [128, 8 * D], BF16)
    W1 = s_w1[:, :].rearrange("p (c l) -> p c l", l=INTER)
    W3 = s_w3[:, :].rearrange("p (c l) -> p c l", l=INTER)
    W2 = s_w2[:, :].rearrange("p (c l) -> p c l", l=D)
    s_ysb = [sbd("s_ysb0", [128, D], BF16), sbd("s_ysb1", [128, D], BF16)]

    hbar = [VST_Z]
    d_lastmm = []
    d_exp_sca = []
    for i in range(EPC):
        GTH0 = gth.n
        DST0 = dst.n
        VSTW0 = vst.n
        idx_ms = []
        wx_ms = []

        def d_bid(ve, i=i):
            ve.wait_ge(vst.h, c_rdy[i])
            vst.inc(ve.tensor_scalar(BID32[:, i, :], BID[:, i, :], 0.0, 127.0,
                                     ALU.max, ALU.min), 1)
        block.vector(d_bid)
        BID_RDY = vst.n

        def d_gth2(gp, i=i, GTH0=GTH0, DST0=DST0, VSTW0=VSTW0, BID_RDY=BID_RDY):
            if i == 0:
                gp.wait_ge(ccs.h, CC_AG)
                gp.wait_ge(zro.h, zro.n)
            gp.wait_ge(vst.h, BID_RDY)
            gp.wait_ge(dst.h, c_rank_st[i])
            for j in range(CAPJ):
                if j >= 2:
                    gp.wait_ge(vst.h, idx_ms[j - 2])
                gth.inc(gp.indirect_dma_start(
                    out=s_grank[j % 2][:, :], out_offset=None,
                    in_=rankT[i].ap(),
                    in_offset=bass.IndirectOffsetOnAxis(BID32[:, i, j:j + 1], 0)), 16)
            if stages <= 4.25:
                return
            for j in range(CAPJ):
                gp.wait_ge(vst.h, idx_ms[j])
                if j >= 2:
                    gp.wait_ge(dst.h, DST0 + (j - 1) * 16)
                    gp.wait_ge(vst.h, wx_ms[j - 2]) if False else None
                    gp.wait_ge(vst.h, wx_ms[j - 2])
                gth.inc(gp.indirect_dma_start(
                    out=s_gbuf[j % 2][:, :], out_offset=None,
                    in_=xnw_all.ap(),
                    in_offset=bass.IndirectOffsetOnAxis(IDXv[:, i, j:j + 1], 0)), 16)

        def d_idx(ve, i=i, GTH0=GTH0):
            for j in range(CAPJ):
                ve.wait_ge(gth.h, GTH0 + (j + 1) * 16)
                ve.tensor_scalar(s_dump[:, 0:256], s_grank[j % 2][:, :],
                                 RELV[:, i, j:j + 1], None, ALU.is_equal)
                ve.tensor_tensor(s_dump[:, 0:256], s_dump[:, 0:256], s_irow[:, :], ALU.mult)
                ve.tensor_reduce(s_dump[:, 256:257], s_dump[:, 0:256], axis=AX.X, op=ALU.add)
                ve.tensor_scalar(s_wred[:, :], BID[:, i, j:j + 1], 256.0, s_dump[:, 256:257],
                                 ALU.mult, ALU.add)
                ve.tensor_scalar(s_wred[:, :], s_wred[:, :], float(TRASH), None, ALU.subtract)
                ve.tensor_tensor(s_wred[:, :], s_wred[:, :], MKV[:, i, j:j + 1], ALU.mult)
                ve.tensor_scalar(s_wred[:, :], s_wred[:, :], float(TRASH), None, ALU.add)
                vst.inc(ve.tensor_scalar(IDXv[:, i, j:j + 1], s_wred[:, :], 0.0,
                                         float(TRASH), ALU.max, ALU.min), 1)
                idx_ms.append(vst.n)
        block.vector(d_idx)

        GTH_X0 = GTH0 + CAPJ * 16

        if stages <= 4.25:
            block.gpsimd(d_gth2)
            continue

        def d_gst(sy, i=i, GTH_X0=GTH_X0):
            for j in range(CAPJ):
                sy.wait_ge(gth.h, GTH_X0 + (j + 1) * 16)
                dst.inc(sy.dma_start(out=xe_dram.ap()[i][j * 128:(j + 1) * 128, :],
                                     in_=s_gbuf[j % 2][:, 0:D]), 16)
        block.sync(d_gst)
        DST_G = dst.n

        def d_wx(ve, i=i, GTH_X0=GTH_X0):
            for j in range(CAPJ):
                ve.wait_ge(gth.h, GTH_X0 + (j + 1) * 16)
                ve.tensor_tensor(s_wtmp[:, :], s_gbuf[j % 2][:, D:ROWW],
                                 ESL[:, i, :], ALU.mult)
                ve.tensor_reduce(s_wred[:, :], s_wtmp[:, :], axis=AX.X, op=ALU.add)
                vst.inc(ve.tensor_tensor(BW[:, i, j:j + 1], s_wred[:, :],
                                         MKV[:, i, j:j + 1], ALU.mult), 2)
                wx_ms.append(vst.n)
        block.vector(d_wx)
        VST_W = vst.n
        block.gpsimd(d_gth2)

        if stages <= 4.5:
            continue

        def d_ldw(gp, i=i):
            if i >= 1:
                gp.wait_ge(tmm.h, d_lastmm[i - 1])
            for dstv, srct in ((W1, w1e), (W3, w3e), (W2, w2e)):
                gld.inc(gp.dma_start(out=dstv,
                                     in_=srct.ap()[i].rearrange("(c p) l -> p c l", p=128)), 16)
        block.gpsimd(d_ldw)
        GLD_E = gld.n

        def d_trx(sy, i=i, DST_G=DST_G):
            sy.wait_ge(dst.h, DST_G)
            if i >= 1:
                sy.wait_ge(tmm.h, d_lastmm[i - 1])
            for c in range(8):
                trx.inc(sy.dma_start_transpose(XE[:, c, :],
                                               xe_dram.ap()[i][:, c * 128:(c + 1) * 128]), 16)
        block.sync(d_trx)
        TRX_D = trx.n

        if stages <= 4.7:
            d_lastmm.append(tmm.n)
            continue
        d_ysc = []
        d_yev_l = []
        d_hloc = []
        for g in range(CAP // 512):
            for ich in range(8):
                def d_hmm(te, i=i, g=g, ich=ich, TRX_D=TRX_D, GLD_E=GLD_E):
                    if g == 0 and ich == 0:
                        te.wait_ge(trx.h, TRX_D)
                        te.wait_ge(gld.h, GLD_E)
                    te.wait_ge(vst.h, hbar[-1])
                    for dch in range(8):
                        mm_ = te.matmul(p_a[:, :], W1[:, dch, ich * 128:(ich + 1) * 128],
                                  XE[:, dch, g * 512:(g + 1) * 512],
                                  start=(dch == 0), stop=(dch == 7))
                    tmm.inc(mm_, 1)
                    for dch in range(8):
                        mm_ = te.matmul(p_b[:, :], W3[:, dch, ich * 128:(ich + 1) * 128],
                                  XE[:, dch, g * 512:(g + 1) * 512],
                                  start=(dch == 0), stop=(dch == 7))
                    tmm.inc(mm_, 1)
                block.tensor(d_hmm)
                TMH2 = tmm.n

                def d_sil(sc, TMH2=TMH2):
                    sc.wait_ge(tmm.h, TMH2 - 1)
                    sst.inc(sc.activation(s_hsil[:, :], p_a[:, :], AF.Silu), 1)
                block.scalar(d_sil)
                SSH2 = sst.n

                def d_mul(ve, ich=ich, TMH2=TMH2, SSH2=SSH2):
                    ve.wait_ge(tmm.h, TMH2)
                    ve.wait_ge(sst.h, SSH2)
                    vst.inc(ve.tensor_tensor(HA[:, ich, :], p_b[:, :], s_hsil[:, :], ALU.mult), 1)
                block.vector(d_mul)
                hbar.append(vst.n)
                d_hloc.append(vst.n)

            for sl in range(4):
                gs = g * 4 + sl

                def d_ymm(te, i=i, g=g, sl=sl, gs=gs):
                    te.wait_ge(vst.h, d_hloc[g * 8 + 7])
                    if gs >= 1:
                        te.wait_ge(vst.h, d_yev_l[gs - 1])
                    if gs >= 2:
                        te.wait_ge(sca.h, d_ysc[gs - 2])
                    for dh in range(2):
                        pt = p_s0 if dh == 0 else p_s1
                        for ich in range(8):
                            mm_ = te.matmul(pt[:, :], HA[:, ich, sl * 128:(sl + 1) * 128],
                                      W2[:, ich, dh * 512:(dh + 1) * 512],
                                      start=(ich == 0), stop=(ich == 7))
                    tmm.inc(mm_, 1)
                block.tensor(d_ymm)
                TMY = tmm.n

                def d_yev(ve, i=i, gs=gs, TMY=TMY, VST_W=VST_W):
                    ve.wait_ge(tmm.h, TMY)
                    ve.wait_ge(vst.h, VST_W)
                    if gs >= 2:
                        ve.wait_ge(sca.h, d_ysc[gs - 2])
                    ve.tensor_scalar(s_ysb[gs % 2][:, 0:512], p_s0[:, :],
                                     BW[:, i, gs:gs + 1], None, ALU.mult)
                    vst.inc(ve.tensor_scalar(s_ysb[gs % 2][:, 512:1024], p_s1[:, :],
                                             BW[:, i, gs:gs + 1], None, ALU.mult), 1)
                block.vector(d_yev)
                d_yev_l.append(vst.n)
                VSTY = vst.n

                def d_sca(gp, i=i, gs=gs, VSTY=VSTY):
                    if stages <= 4.9:
                        return
                    gp.wait_ge(vst.h, VSTY)
                    if i > 0 and gs == 0:
                        gp.wait_ge(sca.h, d_exp_sca[i - 1])
                    if i == 0 and gs == 0:
                        gp.wait_ge(zro.h, zro.n)
                    sca.inc(gp.indirect_dma_start(
                        out=y_acc.ap(),
                        out_offset=bass.IndirectOffsetOnAxis(IDXv[:, i, gs:gs + 1], 0),
                        in_=s_ysb[gs % 2][:, :], in_offset=None,
                        compute_op=ALU.add), 16)
                block.gpsimd(d_sca)
                d_ysc.append(sca.n)
        d_lastmm.append(tmm.n)
        d_exp_sca.append(sca.n)

    if stages <= 5:
        return _quiesce()
    barrier()
    d_ctx.close()
    cd_ctx.close()
    hact_ctx.close()

    e_ctx = ExitStack()
    sbe = lambda n_, s_, d_: e_ctx.enter_context(nc.sbuf_tensor(n_, s_, d_))
    s_yc = [sbe("s_yc0", [128, D], BF16), sbe("s_yc1", [128, D], BF16)]
    s_zc2 = [sbe("s_zc20", [128, D], BF16), sbe("s_zc21", [128, D], BF16)]
    s_x2l = [sbe("s_x2l0", [128, D], F32), sbe("s_x2l1", [128, D], F32)]
    s_ffn = sbe("s_ffn", [128, D], F32)
    s_l4 = sbe("s_l4", [128, D], F32)

    SCA_ALL = sca.n
    DST_ALL = dst.n

    def e_rs(gp, SCA_ALL=SCA_ALL):
        gp.wait_ge(sca.h, SCA_ALL)
        ccs.inc(gp.collective_compute(
            "ReduceScatter", ALU.add, replica_groups=rg,
            ins=[y_acc.ap()[0:NTOK, :].opt()], outs=[y_own.ap().opt()]), 1)
    block.gpsimd(e_rs)
    CC_RS = ccs.n

    e_out = []
    for ch in range(NCH):
        def e_ld(sy, ch=ch, DST_ALL=DST_ALL):
            if ch == 0:
                sy.wait_ge(ccs.h, CC_RS)
                sy.wait_ge(dst.h, DST_ALL)
            if ch >= 2:
                sy.wait_ge(sout.h, (ch - 1) * 16)
            r0 = ch * 128
            ldx.inc(sy.dma_start(out=s_yc[ch % 2][:, :], in_=y_own[r0:r0 + 128, :]), 16)
            ldx.inc(sy.dma_start(out=s_zc2[ch % 2][:, :], in_=z_dram[r0:r0 + 128, :]), 16)
            ldx.inc(sy.dma_start(out=s_x2l[ch % 2][:, :], in_=x2_dram[r0:r0 + 128, :]), 16)
        block.sync(e_ld)
        LDX_E = ldx.n

        def e_v1(ve, ch=ch, LDX_E=LDX_E):
            ve.wait_ge(ldx.h, LDX_E)
            vst.inc(ve.tensor_tensor(s_ffn[:, :], s_yc[ch % 2][:, :], s_zc2[ch % 2][:, :],
                                     ALU.add), 1)
        block.vector(e_v1)
        VFF = vst.n

        emit_ln([s_ffn[:, :]],
                pre_wait_v=lambda ve, t=VFF: ve.wait_ge(vst.h, t),
                pre_wait_s=lambda sc, t=VFF: sc.wait_ge(vst.h, t))

        def e_v2(ve, ch=ch):
            if ch >= 1:
                ve.wait_ge(sout.h, ch * 16)
            ve.tensor_scalar(s_l4[:, :], s_ffn[:, :], s_stat[:, 0:1], s_stat[:, 1:2],
                             ALU.add, ALU.mult)
            vst.inc(ve.tensor_add(s_l4[:, :], s_l4[:, :], s_x2l[ch % 2][:, :]), 1)
        block.vector(e_v2)
        VOUT = vst.n

        def e_st(sy, ch=ch, VOUT=VOUT):
            sy.wait_ge(vst.h, VOUT)
            sout.inc(sy.dma_start(out=out_t[ch * 128:(ch + 1) * 128, :], in_=s_l4[:, :]), 16)
        block.sync(e_st)
        e_out.append(sout.n)

    SOUT_ALL = sout.n

    def fin_g(gp, SOUT_ALL=SOUT_ALL):
        gp.wait_ge(sout.h, SOUT_ALL)
    block.gpsimd(fin_g)

    def fin_s(sy, SOUT_ALL=SOUT_ALL):
        sy.wait_ge(sout.h, SOUT_ALL)
    block.sync(fin_s)

    block_cm.__exit__(None, None, None)
    e_ctx.close()
    nc._keepalive = root
    return nc



def _host_constants(core):
    f32 = np.float32
    bf = ml_dtypes.bfloat16
    c = {}
    j = np.arange(16, dtype=np.float64)
    inv = 10000.0 ** (-2.0 * j / 32.0)
    ang = np.arange(S, dtype=np.float64)[None, :] * inv[:, None]
    c["cosq"] = np.tile(np.cos(ang), (8, 1)).astype(bf)
    c["sinq"] = np.tile(np.sin(ang), (8, 1)).astype(bf)
    c["trimask"] = np.triu(np.ones((128, 128), f32)).astype(bf)
    c["onesf"] = np.ones((1, 128), f32)
    c["onescol"] = np.ones((128, 1), f32)
    c["slotiota"] = np.ascontiguousarray(
        np.arange(CAP, dtype=np.int64).astype(f32).reshape(CAPJ, 128).T)
    c["identf"] = np.eye(128, dtype=f32)
    c["irow"] = np.ascontiguousarray(np.tile(np.arange(256, dtype=f32)[None, :], (128, 1)))
    c["brow128"] = np.arange(128, dtype=f32).reshape(1, 128)
    es = np.zeros((EPC, E), f32)
    for i in range(EPC):
        es[i, EPC * core + i] = 1.0
    c["esel"] = np.ascontiguousarray(np.tile(es.reshape(1, EPC * E), (128, 1)))
    return c


def _perm_qr_cols():
    perm = np.zeros(H * DHR, np.int64)
    for g in range(4):
        basec = g * 128
        for hl in range(4):
            h = 4 * g + hl
            for jj in range(16):
                perm[basec + hl * 16 + jj] = h * DHR + 2 * jj
                perm[basec + 64 + hl * 16 + jj] = h * DHR + 2 * jj + 1
    return perm


_BUILD_CACHE = {}


def kernel(L0_attn_x, L0_attn_W_DQ, L0_attn_W_UQ, L0_attn_W_QR, L0_attn_W_DKV,
           L0_attn_W_UK, L0_attn_W_UV, L0_attn_W_KR, L0_attn_W_O,
           L0_ffn_wg, L0_ffn_w1, L0_ffn_w2, L0_ffn_w3,
           L0_ffn_sw1, L0_ffn_sw2, L0_ffn_sw3):
    f32 = np.float32
    asf = lambda a: np.ascontiguousarray(np.asarray(a), dtype=f32)

    x = asf(L0_attn_x).reshape(B, S, D)
    wdq_t = np.ascontiguousarray(asf(L0_attn_W_DQ).T)
    wdkv_t = np.ascontiguousarray(asf(L0_attn_W_DKV).T)
    wuq_t = np.ascontiguousarray(asf(L0_attn_W_UQ).T)
    wuk_t = np.ascontiguousarray(asf(L0_attn_W_UK).T)
    wuv_t = np.ascontiguousarray(asf(L0_attn_W_UV).T)
    wo_t = np.ascontiguousarray(asf(L0_attn_W_O).T)
    perm = _perm_qr_cols()
    wqr_t = np.ascontiguousarray(asf(L0_attn_W_QR).T[:, perm])
    kperm = np.concatenate([np.arange(0, DHR, 2), np.arange(1, DHR, 2)])
    wkr_t = np.ascontiguousarray(asf(L0_attn_W_KR).T[:, kperm])
    wg_t = np.ascontiguousarray(asf(L0_ffn_wg).T)
    sw1 = asf(L0_ffn_sw1)
    sw3 = asf(L0_ffn_sw3)
    sw2 = asf(L0_ffn_sw2)
    w1 = asf(L0_ffn_w1)
    w3 = asf(L0_ffn_w3)
    w2 = asf(L0_ffn_w2)

    if "nc" not in _BUILD_CACHE:
        _BUILD_CACHE["nc"] = build_program()
    nc = _BUILD_CACHE["nc"]
    if os.environ.get("BASS_SIM_FP64") != "1" and not nc.is_finalized():
        nc.finalize()

    in_maps = []
    for core in range(NCORES):
        m = _host_constants(core)
        m.update(dict(
            x=np.ascontiguousarray(x[BPC * core:BPC * (core + 1)].reshape(TPC, D)),
            wdq_t=wdq_t, wdkv_t=wdkv_t, wkr_t=wkr_t, wuq_t=wuq_t, wqr_t=wqr_t,
            wuk_t=wuk_t, wuv_t=wuv_t, wo_t=wo_t, wg_t=wg_t,
            sw1=sw1, sw3=sw3, sw2=sw2,
            w1e=np.ascontiguousarray(w1[EPC * core:EPC * (core + 1)]),
            w3e=np.ascontiguousarray(w3[EPC * core:EPC * (core + 1)]),
            w2e=np.ascontiguousarray(w2[EPC * core:EPC * (core + 1)]),
        ))
        in_maps.append(m)

    res = run_bass_kernel_spmd(nc, in_maps, core_ids=list(range(NCORES)))
    out = np.concatenate([r["out"].reshape(BPC, S, D) for r in res.results], axis=0)
    return np.ascontiguousarray(out.astype(np.float32).reshape(B, S, D))
